# version 17
# baseline (speedup 1.0000x reference)
import contextlib
import sys
import numpy as np
import ml_dtypes
import concourse.bass as bass
import concourse.mybir as mybir
from concourse.bass_utils import run_bass_kernel_spmd

F32 = mybir.dt.float32
BF16 = mybir.dt.bfloat16
AF = mybir.ActivationFunctionType
ALU = mybir.AluOpType
AX = mybir.AxisListType

S = 2048
D = 4096
H = 32
DN = 128
DV = 128
DQ = 192
S4 = float(DQ) ** -0.25
NTT = S // 512
NKC = D // 128

ENGINES = ("sync", "scalar", "vector", "tensor", "gpsimd")
ALL_SEMS = ("mm", "act", "vec", "ltr0", "ltr1", "lvt0", "lvt1",
            "lid", "lmk", "lb12", "lb3",
            "lx0", "lx1", "lk", "lq", "lv",
            "sa0", "sa1", "lw0", "lw1", "sx0", "sx1", "sx2", "sx3",
            "lw3a", "lw3b", "lw3c", "lw3d", "lx4a", "lx4b", "lxt0", "lxt1",
            "sy0", "sy1", "sy2", "sy3",
            "lwoa", "lwob", "lwoc", "lwod", "laTa", "laTb", "ly0", "ly1",
            "so0", "so1", "so2", "so3")


class Emitter:
    def __init__(self):
        self.cnt = {s: 0 for s in ALL_SEMS}
        self.items = {e: [] for e in ENGINES}
        self.all_items = {e: [] for e in ENGINES}
        self.guaranteed = {e: {} for e in ENGINES}
        self.pending_barrier = None
        self.barrier_done = set()

    def phase_barrier(self):
        self.pending_barrier = dict(self.cnt)
        self.barrier_done = set()

    def op(self, eng, fn, waits=(), inc=None, amt=1):
        waits = list(waits)
        if self.pending_barrier is not None and eng not in self.barrier_done:
            waits += [(s, v) for s, v in self.pending_barrier.items() if v > 0]
            self.barrier_done.add(eng)
            if len(self.barrier_done) == len(ENGINES):
                self.pending_barrier = None
        w = []
        g = self.guaranteed[eng]
        for s, v in waits:
            if v is None or v <= 0:
                continue
            if g.get(s, 0) >= v:
                continue
            g[s] = v
            w.append((s, v))
        if inc is not None:
            self.cnt[inc] += amt
            ret = self.cnt[inc]
        else:
            ret = None
        item = (fn, tuple(w), inc, amt, sys._getframe(1).f_lineno)
        self.items[eng].append(item)
        self.all_items[eng].append(item)
        return ret

    def flush(self, nc, sems):
        if not any(self.items.values()):
            return
        with nc.Block() as block:
            for eng in ENGINES:
                items = self.items[eng]
                if not items:
                    continue

                def body(e, items=items):
                    for fn, waits, inc, amt, _ln in items:
                        for s, v in waits:
                            e.wait_ge(sems[s], v)
                        ins = fn(e) if fn is not None else None
                        if inc is not None:
                            ins.then_inc(sems[inc], amt)

                getattr(block, eng)(body)
        self.items = {e: [] for e in ENGINES}

    def check_deadlock(self):
        cnt = {s: 0 for s in self.cnt}
        heads = {e: 0 for e in ENGINES}
        streams = self.all_items
        while True:
            progress = False
            done = True
            for e in ENGINES:
                st = streams[e]
                while heads[e] < len(st):
                    fn, waits, inc, amt, _ln = st[heads[e]]
                    if all(cnt[s] >= v for s, v in waits):
                        if inc is not None:
                            cnt[inc] += amt
                        heads[e] += 1
                        progress = True
                    else:
                        break
                if heads[e] < len(st):
                    done = False
            if done:
                return
            if not progress:
                info = []
                for e in ENGINES:
                    if heads[e] < len(streams[e]):
                        fn, waits, inc, amt, ln = streams[e][heads[e]]
                        blocked = [(s, v, cnt[s]) for s, v in waits if cnt[s] < v]
                        info.append(f"{e}@{heads[e]}/{len(streams[e])} line{ln} blocked on {blocked}")
                raise RuntimeError("DEADLOCK in emitted program: " + "; ".join(info))


def build_nc():
    nc = bass.Bass("TRN2", target_bir_lowering=False)

    x_h = nc.dram_tensor("x", [S, D], F32, kind="ExternalInput").ap()
    wq_h = nc.dram_tensor("wq_t", [128, H, NKC, 128], F32, kind="ExternalInput").ap()
    wk_h = nc.dram_tensor("wk_t", [128, H, NKC, 128], F32, kind="ExternalInput").ap()
    wv_h = nc.dram_tensor("wv_t", [128, H, NKC, 128], F32, kind="ExternalInput").ap()
    w12_h = nc.dram_tensor("w12_t", [128, 32, NKC, 256], F32, kind="ExternalInput").ap()
    b12_h = nc.dram_tensor("b12_t", [128, 64], F32, kind="ExternalInput").ap()
    w3_h = nc.dram_tensor("w3_t", [128, 4, NKC, 1024], F32, kind="ExternalInput").ap()
    wo_h = nc.dram_tensor("wo_t", [128, 4, H, 1024], F32, kind="ExternalInput").ap()
    b3_h = nc.dram_tensor("b3_t", [1, D], F32, kind="ExternalInput").ap()
    masks_h = nc.dram_tensor("masks_t", [128, 4 * 512], BF16, kind="ExternalInput").ap()
    ident_h = nc.dram_tensor("ident_t", [128, 128], BF16, kind="ExternalInput").ap()
    out_h = nc.dram_tensor("out", [S, D], F32, kind="ExternalOutput").ap()
    attn_scr = nc.dram_tensor("attn_scr", [128, H, S], BF16, kind="Internal").ap()
    x4_scr = nc.dram_tensor("x4_scr", [128, 32, S], BF16, kind="Internal").ap()
    y1_scr = nc.dram_tensor("y1_scr", [S, D], F32, kind="Internal").ap()

    em = Emitter()
    sems = {}
    stack = contextlib.ExitStack()
    for s in ALL_SEMS:
        sems[s] = stack.enter_context(nc.semaphore(s))

    with stack:
        with contextlib.ExitStack() as _pc:
            _e = _pc.enter_context
            ident = _e(nc.sbuf_tensor("ident", [128, 128], BF16))
            ones_col = _e(nc.sbuf_tensor("ones_col", [128, 1], BF16))
            ones_mat = _e(nc.sbuf_tensor("ones_mat", [128, 128], BF16))
            ones_row = _e(nc.sbuf_tensor("ones_row", [1, 128], F32))
            epsb = _e(nc.sbuf_tensor("epsb", [128, 1], F32))
            ld_ident = em.op("sync", lambda e: e.dma_start(ident[:, :], ident_h), inc="lid", amt=16)
            em.op("vector", lambda e: e.memset(ones_col[:, :], 1.0), inc="vec")
            em.op("vector", lambda e: e.memset(ones_mat[:, :], 1.0), inc="vec")
            em.op("vector", lambda e: e.memset(ones_row[:, :], 1.0), inc="vec")
            ms_eps = em.op("vector", lambda e: e.memset(epsb[:, :], 1e-5), inc="vec")

            with nc.sbuf_tensor("xnT", [128, NKC, S], BF16) as xnT:
                with contextlib.ExitStack() as _p0:
                    _e = _p0.enter_context
                    x_in = _e(nc.sbuf_tensor("x_in", [128, 2, D], F32))
                    xc = _e(nc.sbuf_tensor("xc", [128, D], F32))
                    xn_bf = _e(nc.sbuf_tensor("xn_bf", [128, 2, D], BF16))
                    stat = _e(nc.sbuf_tensor("stat", [128, 8], F32))
                    ptrA = _e(nc.psum_tensor("ptrA", [128, 1024], BF16))
                    ptrB = _e(nc.psum_tensor("ptrB", [128, 1024], BF16))

                    xsum = stat[:, 0:1]
                    mneg = stat[:, 1:2]
                    ssq = stat[:, 2:3]
                    sd = stat[:, 3:4]
                    rstd = stat[:, 4:5]
                    xin_free_v = [0, 0]
                    xin_free_a = [0, 0]
                    xnbf_free = [0, 0]
                    ptr_free = {}
                    prev_a3 = 0

                    for i in range(16):
                        par = i % 2
                        ldv = em.op(
                            "sync",
                            lambda e, par=par, i=i: e.dma_start(
                                x_in[:, par], x_h[i * 128:(i + 1) * 128, :]),
                            waits=[("vec", xin_free_v[par]), ("act", xin_free_a[par])],
                            inc="lx%d" % par, amt=16,
                        )
                        v1 = em.op(
                            "vector",
                            lambda e, par=par: e.reduce_sum(xsum, x_in[:, par], axis=AX.X),
                            waits=[("lx%d" % par, ldv), ("act", prev_a3), ("vec", ms_eps)],
                            inc="vec",
                        )
                        a1 = em.op(
                            "scalar",
                            lambda e: e.activation(mneg, xsum, AF.Copy, scale=-1.0 / D),
                            waits=[("vec", v1)],
                            inc="act",
                        )
                        v2 = em.op(
                            "vector",
                            lambda e, par=par: e.tensor_scalar(xc[:, :], x_in[:, par], mneg, None, ALU.add),
                            waits=[("act", a1)],
                            inc="vec",
                        )
                        xin_free_v[par] = v2
                        a2 = em.op(
                            "scalar",
                            lambda e, par=par: e.activation(x_in[:, par], xc[:, :], AF.Square, accum_out=ssq),
                            waits=[("vec", v2)],
                            inc="act",
                        )
                        xin_free_a[par] = a2
                        a3 = em.op(
                            "scalar",
                            lambda e: e.activation(sd, ssq, AF.Sqrt, scale=1.0 / D, bias=epsb[:, :]),
                            waits=[("act", a2)],
                            inc="act",
                        )
                        prev_a3 = a3
                        v3 = em.op(
                            "vector",
                            lambda e: e.reciprocal(rstd, sd),
                            waits=[("act", a3)],
                            inc="vec",
                        )
                        v4 = em.op(
                            "vector",
                            lambda e, par=par: e.tensor_scalar(xn_bf[:, par], xc[:, :], rstd, None, ALU.mult),
                            waits=[("vec", v3), ("act", xnbf_free[par])],
                            inc="vec",
                        )
                        last_t = 0
                        for g in range(NKC // 4):
                            gb = g % 2
                            bank = (ptrA, ptrB)[gb]
                            tg = 0
                            for slot in range(4):
                                kc = g * 4 + slot
                                tg = em.op(
                                    "tensor",
                                    lambda e, bank=bank, slot=slot, par=par, kc=kc: e.transpose(
                                        bank[:, slot * 128:(slot + 1) * 128],
                                        xn_bf[:, par, kc * 128:(kc + 1) * 128],
                                        ident[:, :],
                                    ),
                                    waits=([("vec", v4), ("lid", ld_ident),
                                            ("act", ptr_free.get(gb, 0))] if slot == 0 else ()),
                                    inc="mm" if slot == 3 else None,
                                )
                            ev = em.op(
                                "scalar",
                                lambda e, bank=bank, g=g, i=i: e.copy(
                                    xnT[:, g * 4:(g + 1) * 4, i * 128:(i + 1) * 128],
                                    bank[:, 0:512].rearrange("p (a b) -> p a b", a=4),
                                ),
                                waits=[("mm", tg)],
                                inc="act",
                            )
                            ptr_free[gb] = ev
                            last_t = ev
                        xnbf_free[par] = last_t
                    xnT_done_act = em.cnt["act"]
                em.flush(nc, sems)
                em.phase_barrier()

                with contextlib.ExitStack() as _p1:
                    _e = _p1.enter_context
                    masks = _e(nc.sbuf_tensor("masks", [128, 4 * 512], BF16))
                    ld_masks = em.op("sync", lambda e: e.dma_start(masks[:, :], masks_h),
                                     inc="lmk", amt=16)
                    k_w = _e(nc.sbuf_tensor("k_w", [128, NKC, 128], BF16))
                    q_w = _e(nc.sbuf_tensor("q_w", [128, NKC, 128], BF16))
                    v_w = _e(nc.sbuf_tensor("v_w", [128, NKC, 128], BF16))
                    kT_sb = _e(nc.sbuf_tensor("kT_sb", [128, S], BF16))
                    qT_sb = _e(nc.sbuf_tensor("qT_sb", [128, S], BF16))
                    V_sb = _e(nc.sbuf_tensor("V_sb", [128, S], BF16))
                    sq_sb = _e(nc.sbuf_tensor("sq_sb", [128, S], BF16))
                    tmp4 = _e(nc.sbuf_tensor("tmp4", [128, 2, 512], BF16))
                    expS = _e(nc.sbuf_tensor("expS", [128, 8, 512], BF16))
                    expT = _e(nc.sbuf_tensor("expT", [128, 4, 512], BF16))
                    attnT_st = _e(nc.sbuf_tensor("attnT_st", [128, 2, 512], BF16))
                    recipb = _e(nc.sbuf_tensor("recipb", [128, 512], F32))
                    lnr = _e(nc.sbuf_tensor("lnr", [128, 512], F32))
                    negc = _e(nc.sbuf_tensor("negc", [128, 1], F32))
                    frags = _e(nc.sbuf_tensor("frags", [128, 16], F32))
                    csc = _e(nc.sbuf_tensor("csc", [128, 4], F32))
                    pk0 = _e(nc.psum_tensor("pk0", [128, 512], F32))
                    pk1 = _e(nc.psum_tensor("pk1", [128, 512], F32))
                    ps0 = _e(nc.psum_tensor("ps0", [128, 512], F32))
                    ps1 = _e(nc.psum_tensor("ps1", [128, 512], F32))
                    ppv = _e(nc.psum_tensor("ppv", [128, 512], F32))
                    psb = _e(nc.psum_tensor("psb", [128, 512], F32))
                    ptrV = _e(nc.psum_tensor("ptrV", [128, 1024], BF16))
                    ps3 = _e(nc.psum_tensor("ps3", [128, 512], F32))
                    w_free = {"k": 0, "q": 0, "v": 0}
                    w_ready = {"k": 0, "q": 0, "v": 0}
                    evac_free = {0: 0, 1: 0}
                    vsb_war = [0]
                    vtr_free = {0: 0}
                    tmp4_free = [0, 0]
                    attnT_free = [0, 0]
                    state = {"recipb_war": 0, "psb_free_vec": 0, "psb_free_act": 0}

                    def wload(name, hbm_ap, sb, h):
                        return em.op(
                            "gpsimd",
                            lambda e, hbm_ap=hbm_ap, sb=sb, h=h: e.dma_start(sb[:, :, :], hbm_ap[:, h]),
                            waits=[("mm", w_free[name])],
                            inc="l" + name, amt=16,
                        )

                    w_ready["k"] = wload("k", wk_h, k_w, 0)
                    w_ready["q"] = wload("q", wq_h, q_w, 0)
                    w_ready["v"] = wload("v", wv_h, v_w, 0)

                    def qkv_block(w_sb, rsem, ready, dst, scale, extra_waits):
                        stop = 0
                        evacs = [0] * NTT
                        for tt in range(NTT):
                            bank = (pk0, pk1)[tt % 2]
                            mv = None
                            for kc in range(NKC):
                                mv = em.op(
                                    "tensor",
                                    lambda e, bank=bank, kc=kc, tt=tt, w_sb=w_sb: e.matmul(
                                        bank[:, :], w_sb[:, kc, :],
                                        xnT[:, kc, tt * 512:(tt + 1) * 512],
                                        start=(kc == 0), stop=(kc == NKC - 1)),
                                    waits=(([(rsem, ready), ("act", xnT_done_act),
                                             ("act", evac_free[tt % 2])]
                                            + (extra_waits if tt == 0 else []))
                                           if kc == 0 else ()),
                                    inc="mm" if kc == NKC - 1 else None,
                                )
                            stop = mv
                            if dst is not None:
                                evacs[tt] = em.op(
                                    "scalar",
                                    lambda e, bank=bank, tt=tt, dst=dst, scale=scale: e.activation(
                                        dst[:, tt * 512:(tt + 1) * 512], bank[:, :],
                                        AF.Copy, scale=scale),
                                    waits=[("mm", mv)],
                                    inc="act",
                                )
                                evac_free[tt % 2] = evacs[tt]
                            else:
                                tpar = tt % 2
                                ev = em.op(
                                    "scalar",
                                    lambda e, bank=bank, tpar=tpar: e.copy(tmp4[:, tpar], bank[:, :]),
                                    waits=[("mm", mv), ("mm", tmp4_free[tpar])],
                                    inc="act",
                                )
                                evac_free[tt % 2] = ev
                                tg = 0
                                for i4 in range(4):
                                    tg = em.op(
                                        "tensor",
                                        lambda e, tpar=tpar, i4=i4: e.transpose(
                                            ptrV[:, i4 * 128:(i4 + 1) * 128],
                                            tmp4[:, tpar, i4 * 128:(i4 + 1) * 128],
                                            ident[:, :]),
                                        waits=([("act", ev), ("act", vtr_free[0]),
                                                ("mm", vsb_war[0])] if i4 == 0 else ()),
                                        inc="mm" if i4 == 3 else None,
                                    )
                                tmp4_free[tpar] = tg
                                vev = em.op(
                                    "scalar",
                                    lambda e, tt=tt: e.copy(
                                        V_sb[:, tt * 512:(tt + 1) * 512],
                                        ptrV[:, 0:512]),
                                    waits=[("mm", tg)],
                                    inc="act",
                                )
                                vtr_free[0] = vev
                        return stop, evacs

                    kk_banks = (ps0, ps1, ppv, psb)

                    def norm_bound(src, src_evacs, frag_base, extra_mm_wait, bank_prev_fv,
                                   extra_waits):
                        sqv = [0] * NTT
                        for tt in range(NTT):
                            sqv[tt] = em.op(
                                "vector",
                                lambda e, tt=tt, src=src: e.tensor_mul(
                                    sq_sb[:, tt * 512:(tt + 1) * 512],
                                    src[:, tt * 512:(tt + 1) * 512],
                                    src[:, tt * 512:(tt + 1) * 512]),
                                waits=[("act", src_evacs[tt]), ("mm", extra_mm_wait)],
                                inc="vec",
                            )
                        mms = [0] * NTT
                        for tt in range(NTT):
                            mms[tt] = em.op(
                                "tensor",
                                lambda e, tt=tt: e.matmul(
                                    kk_banks[tt][:, :], ones_mat[:, :],
                                    sq_sb[:, tt * 512:(tt + 1) * 512], start=True, stop=True),
                                waits=[("vec", sqv[tt]), ("vec", bank_prev_fv[tt])] + extra_waits,
                                inc="mm",
                            )
                        fv = [0] * NTT
                        for tt in range(NTT):
                            fv[tt] = em.op(
                                "vector",
                                lambda e, tt=tt, frag_base=frag_base: e.reduce_max(
                                    frags[:, frag_base + tt:frag_base + tt + 1],
                                    kk_banks[tt][:, :], axis=AX.X),
                                waits=[("mm", mms[tt])],
                                inc="vec",
                            )
                        return mms, fv

                    sbank_free_vec = [0, 0]
                    expT_free_v = 0
                    prev_fv = [0] * NTT
                    prev_qqm = 0
                    sb_rot = (ps0, ps1, pk0, ps3)
                    NSB = 4
                    pv_rot = (ppv, pk1)
                    sb_free_act = {0: 0, 1: 0, 2: 0, 3: 0}
                    sb_free_vec = {0: 0, 1: 0, 2: 0, 3: 0}
                    pv_free = {0: 0, 1: 0}
                    pv_free_act = {0: 0, 1: 0}
                    expT_free_v = [0, 0, 0, 0]
                    l1_free = {0: 0}
                    prev_fv = [0] * NTT
                    prev_qqm = 0
                    qtg = 0

                    for h in range(H):
                        k_extra = [("act", sb_free_act[2]), ("vec", pv_free[0]),
                                   ("vec", pv_free[1])]
                        k_stop, k_evac = qkv_block(k_w, "lk", w_ready["k"], kT_sb, S4, k_extra)
                        w_free["k"] = k_stop
                        if h + 1 < H:
                            k_next = wload("k", wk_h, k_w, h + 1)
                        q_stop, q_evac = qkv_block(q_w, "lq", w_ready["q"], qT_sb, S4, [])
                        w_free["q"] = q_stop
                        if h + 1 < H:
                            q_next = wload("q", wq_h, q_w, h + 1)
                        bank_extra = [("act", sb_free_act[0]), ("act", sb_free_act[1]),
                                      ("vec", pv_free[0]), ("act", l1_free[0])]
                        kkm, kk_fv = norm_bound(kT_sb, k_evac, 0, prev_qqm, prev_fv, bank_extra)
                        mkk_v = em.op(
                            "vector",
                            lambda e: e.reduce_max(csc[:, 0:1], frags[:, 0:4], axis=AX.X),
                            waits=[("vec", kk_fv[-1])],
                            inc="vec",
                        )
                        v_stop, _ = qkv_block(v_w, "lv", w_ready["v"], None, 1.0, [])
                        w_free["v"] = v_stop
                        if h + 1 < H:
                            v_next = wload("v", wv_h, v_w, h + 1)
                        v_rdy_act = em.cnt["act"]
                        qqm, qq_fv = norm_bound(qT_sb, q_evac, 4, kkm[-1], kk_fv, bank_extra)
                        prev_fv = qq_fv
                        prev_qqm = qqm[-1]
                        sb_free_vec = {0: qq_fv[0], 1: qq_fv[1], 2: 0, 3: 0}
                        sb_free_act[2] = max(sb_free_act[2], evac_free[0])
                        pv_free[0] = max(pv_free[0], qq_fv[2])
                        pv_free_act = {0: 0, 1: evac_free[1]}
                        state["psb_free_vec"] = max(state["psb_free_vec"], qq_fv[3])
                        mqq_v = em.op(
                            "vector",
                            lambda e: e.reduce_max(csc[:, 1:2], frags[:, 4:8], axis=AX.X),
                            waits=[("vec", qq_fv[-1])],
                            inc="vec",
                        )
                        csum_v = em.op(
                            "vector",
                            lambda e: e.tensor_add(csc[:, 2:3], csc[:, 0:1], csc[:, 1:2]),
                            waits=[("vec", mqq_v), ("vec", mkk_v)],
                            inc="vec",
                        )
                        negc_a = em.op(
                            "scalar",
                            lambda e: e.activation(negc[:, :], csc[:, 2:3], AF.Copy, scale=-0.5),
                            waits=[("vec", csum_v)],
                            inc="act",
                        )

                        LOOK = 6
                        for qt in range(NTT):
                            nchunks = (qt + 1) * 4
                            order = list(range(nchunks - 1, -1, -1))
                            pvp = (qt + 1) % 2
                            pvb = pv_rot[pvp]
                            smb = psb
                            rdy_list = [None] * nchunks
                            sums_last = 0
                            pv_last = 0
                            for step in range(nchunks + LOOK):
                                if step < nchunks:
                                    ii = step
                                    kc = order[ii]
                                    b = ii % NSB
                                    sbank = sb_rot[b]
                                    epar = ii % 8
                                    smm = em.op(
                                        "tensor",
                                        lambda e, sbank=sbank, kc=kc, qt=qt: e.matmul(
                                            sbank[:, :], kT_sb[:, kc * 128:(kc + 1) * 128],
                                            qT_sb[:, qt * 512:(qt + 1) * 512],
                                            start=True, stop=True),
                                        waits=[("act", sb_free_act[b]), ("vec", sb_free_vec[b]),
                                               ("act", k_evac[kc // 4]), ("act", q_evac[qt])],
                                        inc="mm",
                                    )
                                    if kc >= qt * 4:
                                        j = kc - qt * 4
                                        etp = ii % 4
                                        ex = em.op(
                                            "scalar",
                                            lambda e, sbank=sbank, etp=etp: e.activation(
                                                expT[:, etp], sbank[:, :], AF.Exp,
                                                bias=negc[:, :]),
                                            waits=[("mm", smm), ("act", negc_a),
                                                   ("vec", expT_free_v[etp])],
                                            inc="act",
                                        )
                                        sb_free_act[b] = ex
                                        rdy = em.op(
                                            "vector",
                                            lambda e, epar=epar, j=j, etp=etp: e.tensor_mul(
                                                expS[:, epar], expT[:, etp],
                                                masks[:, j * 512:(j + 1) * 512]),
                                            waits=[("act", ex), ("lmk", ld_masks)],
                                            inc="vec",
                                        )
                                        expT_free_v[etp] = rdy
                                        rdy_list[ii] = ("vec", rdy)
                                    else:
                                        rdy = em.op(
                                            "scalar",
                                            lambda e, sbank=sbank, epar=epar: e.activation(
                                                expS[:, epar], sbank[:, :], AF.Exp,
                                                bias=negc[:, :]),
                                            waits=[("mm", smm), ("act", negc_a)],
                                            inc="act",
                                        )
                                        sb_free_act[b] = rdy
                                        rdy_list[ii] = ("act", rdy)
                                jj = step - LOOK
                                if jj >= 0:
                                    kcj = order[jj]
                                    epj = jj % 8
                                    first = jj == 0
                                    last = jj == nchunks - 1
                                    sums_last = em.op(
                                        "tensor",
                                        lambda e, epj=epj, first=first, last=last, smb=smb: e.matmul(
                                            smb[:, :], ones_mat[:, :], expS[:, epj],
                                            start=first, stop=last),
                                        waits=([rdy_list[jj], ("act", l1_free[0])] if first
                                               else [rdy_list[jj]]),
                                        inc="mm" if last else None,
                                    )
                                    pv_last = em.op(
                                        "tensor",
                                        lambda e, pvb=pvb, epj=epj, kcj=kcj, first=first,
                                               last=last: e.matmul(
                                            pvb[:, :], V_sb[:, kcj * 128:(kcj + 1) * 128],
                                            expS[:, epj], start=first, stop=last),
                                        waits=([("vec", pv_free[pvp]),
                                                ("act", pv_free_act[pvp]),
                                                ("act", v_rdy_act)]
                                               if first else ()),
                                        inc="mm" if last else None,
                                    )
                            l1 = em.op(
                                "scalar",
                                lambda e, smb=smb: e.activation(lnr[:, :], smb[:, :], AF.Ln),
                                waits=[("mm", sums_last)],
                                inc="act",
                            )
                            l1_free[0] = l1
                            r2 = em.op(
                                "scalar",
                                lambda e: e.activation(recipb[:, :], lnr[:, :], AF.Exp, scale=-1.0),
                                waits=[("act", l1), ("vec", state["recipb_war"])],
                                inc="act",
                            )
                            apar = qt % 2
                            nrm = em.op(
                                "vector",
                                lambda e, apar=apar, pvb=pvb: e.tensor_mul(
                                    attnT_st[:, apar], pvb[:, :], recipb[:, :]),
                                waits=[("act", r2), ("mm", pv_last),
                                       ("sa%d" % apar, attnT_free[apar])],
                                inc="vec",
                            )
                            state["recipb_war"] = nrm
                            pv_free[pvp] = nrm
                            attnT_free[apar] = em.op(
                                "scalar",
                                lambda e, apar=apar, h=h, qt=qt: e.dma_start(
                                    attn_scr[:, h, qt * 512:(qt + 1) * 512], attnT_st[:, apar]),
                                waits=[("vec", nrm)],
                                inc="sa%d" % apar, amt=16,
                            )
                            qtg += 1
                        vsb_war[0] = em.cnt["mm"]
                        if h + 1 < H:
                            w_ready["k"] = k_next
                            w_ready["q"] = q_next
                            w_ready["v"] = v_next
                    p1_sa = (em.cnt["sa0"], em.cnt["sa1"])
                em.flush(nc, sems)
                em.phase_barrier()

                with contextlib.ExitStack() as _p2a:
                    _e = _p2a.enter_context
                    w12_bf = _e(nc.sbuf_tensor("w12_bf", [128, 2, NKC, 256], BF16))
                    b12_sb = _e(nc.sbuf_tensor("b12_sb", [128, 64], F32))
                    x4a = _e(nc.sbuf_tensor("x4a", [128, 4, 512], F32))
                    x4_st = _e(nc.sbuf_tensor("x4_st", [128, 4, 512], BF16))
                    pa0 = _e(nc.psum_tensor("pa0", [128, 512], F32))
                    pa1 = _e(nc.psum_tensor("pa1", [128, 512], F32))
                    pa2 = _e(nc.psum_tensor("pa2", [128, 512], F32))
                    pa3 = _e(nc.psum_tensor("pa3", [128, 512], F32))
                    pb0 = _e(nc.psum_tensor("pb0", [128, 512], F32))
                    pb1 = _e(nc.psum_tensor("pb1", [128, 512], F32))
                    pb2 = _e(nc.psum_tensor("pb2", [128, 512], F32))
                    pb3 = _e(nc.psum_tensor("pb3", [128, 512], F32))
                    ld_b12 = em.op("sync", lambda e: e.dma_start(b12_sb[:, :], b12_h),
                                   inc="lb12", amt=16)
                    w12_free = [0, 0]
                    w12_ready = [0, 0]
                    silu_free = [0, 0, 0, 0]
                    spill_free = [0, 0, 0, 0]
                    pA_free = [0, 0, 0, 0]
                    pB_free = [0, 0, 0, 0]
                    git = 0
                    for j in range(32):
                        par = j % 2
                        w12_ready[par] = em.op(
                            "gpsimd",
                            lambda e, par=par, j=j: e.dma_start(w12_bf[:, par], w12_h[:, j]),
                            waits=[("mm", w12_free[par])],
                            inc="lw%d" % par, amt=16,
                        )
                        w12_last = 0
                        for tt in range(NTT):
                            gpar = git % 4
                            pA = (pa0, pa1, pa2, pa3)[gpar]
                            pB = (pb0, pb1, pb2, pb3)[gpar]
                            m1 = None
                            for kc in range(NKC):
                                m1 = em.op(
                                    "tensor",
                                    lambda e, pA=pA, par=par, kc=kc, tt=tt: e.matmul(
                                        pA[:, :], w12_bf[:, par, kc, 0:128],
                                        xnT[:, kc, tt * 512:(tt + 1) * 512],
                                        start=(kc == 0), stop=(kc == NKC - 1)),
                                    waits=([("lw%d" % par, w12_ready[par]), ("act", pA_free[gpar])]
                                           if kc == 0 else ()),
                                    inc="mm" if kc == NKC - 1 else None,
                                )
                            m2 = None
                            for kc in range(NKC):
                                m2 = em.op(
                                    "tensor",
                                    lambda e, pB=pB, par=par, kc=kc, tt=tt: e.matmul(
                                        pB[:, :], w12_bf[:, par, kc, 128:256],
                                        xnT[:, kc, tt * 512:(tt + 1) * 512],
                                        start=(kc == 0), stop=(kc == NKC - 1)),
                                    waits=([("vec", pB_free[gpar])] if kc == 0 else ()),
                                    inc="mm" if kc == NKC - 1 else None,
                                )
                            w12_last = m2
                            spar = git % 4
                            sa = em.op(
                                "scalar",
                                lambda e, pA=pA, spar=spar, j=j: e.activation(
                                    x4a[:, spar], pA[:, :], AF.Silu,
                                    bias=b12_sb[:, 2 * j:2 * j + 1]),
                                waits=[("mm", m1), ("vec", silu_free[spar]), ("lb12", ld_b12)],
                                inc="act",
                            )
                            pA_free[gpar] = sa
                            sv = em.op(
                                "vector",
                                lambda e, pB=pB, spar=spar, j=j: e.scalar_tensor_tensor(
                                    x4_st[:, spar], pB[:, :], b12_sb[:, 2 * j + 1:2 * j + 2],
                                    x4a[:, spar], ALU.add, ALU.mult),
                                waits=[("mm", m2), ("act", sa), ("sx%d" % spar, spill_free[spar])],
                                inc="vec",
                            )
                            silu_free[spar] = sv
                            pB_free[gpar] = sv
                            spill_free[spar] = em.op(
                                "scalar",
                                lambda e, spar=spar, j=j, tt=tt: e.dma_start(
                                    x4_scr[:, j, tt * 512:(tt + 1) * 512], x4_st[:, spar]),
                                waits=[("vec", sv)],
                                inc="sx%d" % spar, amt=16,
                            )
                            git += 1
                        w12_free[par] = w12_last
                    p2a_sx = tuple(em.cnt["sx%d" % i] for i in range(4))
                em.flush(nc, sems)
                em.phase_barrier()

            with contextlib.ExitStack() as _p2b:
                _e = _p2b.enter_context
                w3_bf = _e(nc.sbuf_tensor("w3_bf", [128, 2, NKC, 512], BF16))
                x4T_tt = _e(nc.sbuf_tensor("x4T_tt", [128, 2, 32, 512], BF16))
                b3b = _e(nc.sbuf_tensor("b3b", [128, 1024], F32))
                b3_row = _e(nc.sbuf_tensor("b3_row", [1, 512], F32))
                x_t = _e(nc.sbuf_tensor("x_t", [128, 512], F32))
                y_st = _e(nc.sbuf_tensor("y_st", [128, 2, 512], F32))
                pe0 = _e(nc.psum_tensor("pe0", [128, 512], F32))
                pe1 = _e(nc.psum_tensor("pe1", [128, 512], F32))
                pe2 = _e(nc.psum_tensor("pe2", [128, 512], F32))
                pe3 = _e(nc.psum_tensor("pe3", [128, 512], F32))
                pbc = _e(nc.psum_tensor("pbc", [128, 512], F32))
                NB = 4
                b3_state = {"a": 0, "m": 0, "war": 0}

                def build_b3b(nb):
                    for k4 in range(2):
                        k8 = nb * 2 + k4
                        ld_b3 = em.op(
                            "sync",
                            lambda e, k8=k8: e.dma_start(
                                b3_row[:, :], b3_h[0:1, k8 * 512:(k8 + 1) * 512]),
                            waits=[("mm", b3_state["m"])],
                            inc="lb3", amt=16)
                        bm = em.op(
                            "tensor",
                            lambda e: e.matmul(
                                pbc[:, :], ones_row[:, :], b3_row[:, :],
                                start=True, stop=True),
                            waits=[("lb3", ld_b3), ("act", b3_state["a"]),
                                   ("vec", b3_state["war"])],
                            inc="mm",
                        )
                        b3_state["m"] = bm
                        b3_state["a"] = em.op(
                            "scalar",
                            lambda e, k4=k4: e.copy(b3b[:, k4 * 512:(k4 + 1) * 512], pbc[:, :]),
                            waits=[("mm", bm)],
                            inc="act",
                        )
                    return b3_state["a"]

                b3b_done = build_b3b(0)
                w3sem = ("lw3a", "lw3b")
                w3_ready = {}
                w3_war = {}
                x4_war = [0, 0]
                x4_ready = [0, 0]
                xt_ready = 0
                xt_free = 0
                y_free = [0, 0]
                pe_free = [0, 0, 0, 0]
                git = 0

                def load_w3_cols(nb, nt):
                    wp = (nb * 2 + nt) % 2
                    last = 0
                    for j in range(NKC):
                        last = em.op(
                            "gpsimd",
                            lambda e, nb=nb, nt=nt, j=j, wp=wp: e.dma_start(
                                w3_bf[:, wp, j, :],
                                w3_h[:, nb, j, nt * 512:(nt + 1) * 512]),
                            waits=[("mm", w3_war.get(wp, 0))],
                            inc=w3sem[wp], amt=16,
                        )
                    w3_ready[(nb, nt)] = last

                load_w3_cols(0, 0)
                last_v1 = 0
                for nb in range(NB):
                    if nb > 0:
                        b3_state["war"] = last_v1
                        b3b_done = build_b3b(nb)
                    for nt in range(2):
                        nxt = (nb, nt + 1) if nt == 0 else (nb + 1, 0)
                        if nxt[0] < NB:
                            load_w3_cols(*nxt)
                        for tt in range(NTT):
                            xp = (((nb * 2 + nt) * NTT) + tt) % 2
                            last = 0
                            for j in range(32):
                                last = em.op(
                                    "sync",
                                    lambda e, j=j, tt=tt, xp=xp: e.dma_start(
                                        x4T_tt[:, xp, j, :], x4_scr[:, j, tt * 512:(tt + 1) * 512]),
                                    waits=[("sx0", p2a_sx[0]), ("sx1", p2a_sx[1]),
                                           ("sx2", p2a_sx[2]), ("sx3", p2a_sx[3]),
                                           ("mm", x4_war[xp])],
                                    inc=("lx4a", "lx4b")[xp], amt=16,
                                )
                            x4_ready[xp] = last
                            for ts in range(4):
                                r0 = tt * 512 + ts * 128
                                xt_ready = em.op(
                                    "sync",
                                    lambda e, r0=r0, nb=nb, nt=nt: e.dma_start(
                                        x_t[:, 0:512],
                                        x_h[r0:r0 + 128,
                                            nb * 1024 + nt * 512:nb * 1024 + (nt + 1) * 512]),
                                    waits=[("vec", xt_free)],
                                    inc="lxt0", amt=16,
                                )
                                gpar = git % 4
                                pe = (pe0, pe1, pe2, pe3)[gpar]
                                mv = None
                                for j in range(NKC):
                                    mv = em.op(
                                        "tensor",
                                        lambda e, pe=pe, j=j, ts=ts, nt=nt, xp=xp: e.matmul(
                                            pe[:, :], x4T_tt[:, xp, j, ts * 128:(ts + 1) * 128],
                                            w3_bf[:, (nb * 2 + nt) % 2, j, :],
                                            start=(j == 0), stop=(j == NKC - 1)),
                                        waits=([(("lx4a", "lx4b")[xp], x4_ready[xp]),
                                                (w3sem[(nb * 2 + nt) % 2], w3_ready[(nb, nt)]),
                                                ("vec", pe_free[gpar])] if j == 0 else ()),
                                        inc="mm" if j == NKC - 1 else None,
                                    )
                                w3_war[(nb * 2 + nt) % 2] = mv
                                x4_war[xp] = mv
                                ypar = git % 2
                                v1 = em.op(
                                    "vector",
                                    lambda e, pe=pe, ypar=ypar, nt=nt: e.tensor_add(
                                        y_st[:, ypar], pe[:, :], b3b[:, nt * 512:(nt + 1) * 512]),
                                    waits=[("mm", mv), ("act", b3b_done),
                                           ("sy%d" % ypar, y_free[ypar])],
                                    inc="vec",
                                )
                                last_v1 = v1
                                v2 = em.op(
                                    "vector",
                                    lambda e, ypar=ypar: e.tensor_add(
                                        y_st[:, ypar], y_st[:, ypar], x_t[:, 0:512]),
                                    waits=[("vec", v1), ("lxt0", xt_ready)],
                                    inc="vec",
                                )
                                pe_free[gpar] = v1
                                xt_free = v2
                                y_free[ypar] = em.op(
                                    "scalar",
                                    lambda e, ypar=ypar, r0=r0, nb=nb, nt=nt: e.dma_start(
                                        y1_scr[r0:r0 + 128,
                                               nb * 1024 + nt * 512: nb * 1024 + (nt + 1) * 512],
                                        y_st[:, ypar]),
                                    waits=[("vec", v2)],
                                    inc="sy%d" % ypar, amt=16,
                                )
                                git += 1
                p2b_sy = tuple(em.cnt["sy%d" % i] for i in range(4))
            em.flush(nc, sems)
            em.phase_barrier()

            with contextlib.ExitStack() as _p3:
                _e = _p3.enter_context
                wo_bf = _e(nc.sbuf_tensor("wo_bf", [128, 2, H, 512], BF16))
                aT_tt = _e(nc.sbuf_tensor("aT_tt", [128, 2, H, 512], BF16))
                y1_t = _e(nc.sbuf_tensor("y1_t", [128, 512], F32))
                o_st = _e(nc.sbuf_tensor("o_st", [128, 2, 512], F32))
                pf0 = _e(nc.psum_tensor("pf0", [128, 512], F32))
                pf1 = _e(nc.psum_tensor("pf1", [128, 512], F32))
                pf2 = _e(nc.psum_tensor("pf2", [128, 512], F32))
                pf3 = _e(nc.psum_tensor("pf3", [128, 512], F32))
                NB = 4
                wosem = ("lwoa", "lwob")
                wo_ready = {}
                wo_war = {}
                a_war = [0, 0]
                a_ready = [0, 0]
                yt_ready = 0
                yt_free = 0
                o_free = [0, 0]
                pf_free = [0, 0, 0, 0]
                git = 0

                def load_wo_cols(nb, nt):
                    wp = (nb * 2 + nt) % 2
                    last = 0
                    for hh in range(H):
                        last = em.op(
                            "gpsimd",
                            lambda e, nb=nb, nt=nt, hh=hh, wp=wp: e.dma_start(
                                wo_bf[:, wp, hh, :],
                                wo_h[:, nb, hh, nt * 512:(nt + 1) * 512]),
                            waits=[("mm", wo_war.get(wp, 0))],
                            inc=wosem[wp], amt=16,
                        )
                    wo_ready[(nb, nt)] = last

                load_wo_cols(0, 0)
                for nb in range(NB):
                    for nt in range(2):
                        nxt = (nb, nt + 1) if nt == 0 else (nb + 1, 0)
                        if nxt[0] < NB:
                            load_wo_cols(*nxt)
                        for tt in range(NTT):
                            xp = (((nb * 2 + nt) * NTT) + tt) % 2
                            last = 0
                            for hh in range(H):
                                last = em.op(
                                    "sync",
                                    lambda e, hh=hh, tt=tt, xp=xp: e.dma_start(
                                        aT_tt[:, xp, hh, :],
                                        attn_scr[:, hh, tt * 512:(tt + 1) * 512]),
                                    waits=[("sa0", p1_sa[0]), ("sa1", p1_sa[1]),
                                           ("mm", a_war[xp])],
                                    inc=("laTa", "laTb")[xp], amt=16,
                                )
                            a_ready[xp] = last
                            for ts in range(4):
                                r0 = tt * 512 + ts * 128
                                yt_ready = em.op(
                                    "sync",
                                    lambda e, r0=r0, nb=nb, nt=nt: e.dma_start(
                                        y1_t[:, 0:512],
                                        y1_scr[r0:r0 + 128,
                                               nb * 1024 + nt * 512:nb * 1024 + (nt + 1) * 512]),
                                    waits=[("vec", yt_free), ("sy0", p2b_sy[0]),
                                           ("sy1", p2b_sy[1]), ("sy2", p2b_sy[2]),
                                           ("sy3", p2b_sy[3])],
                                    inc="ly0", amt=16,
                                )
                                gpar = git % 4
                                pf = (pf0, pf1, pf2, pf3)[gpar]
                                mv = None
                                for hh in range(H):
                                    mv = em.op(
                                        "tensor",
                                        lambda e, pf=pf, hh=hh, ts=ts, nt=nt, xp=xp: e.matmul(
                                            pf[:, :], aT_tt[:, xp, hh, ts * 128:(ts + 1) * 128],
                                            wo_bf[:, (nb * 2 + nt) % 2, hh, :],
                                            start=(hh == 0), stop=(hh == H - 1)),
                                        waits=([(("laTa", "laTb")[xp], a_ready[xp]),
                                                (wosem[(nb * 2 + nt) % 2], wo_ready[(nb, nt)]),
                                                ("vec", pf_free[gpar])] if hh == 0 else ()),
                                        inc="mm" if hh == H - 1 else None,
                                    )
                                wo_war[(nb * 2 + nt) % 2] = mv
                                a_war[xp] = mv
                                opar = git % 2
                                v1 = em.op(
                                    "vector",
                                    lambda e, pf=pf, opar=opar: e.tensor_add(
                                        o_st[:, opar], pf[:, :], y1_t[:, 0:512]),
                                    waits=[("mm", mv), ("ly0", yt_ready),
                                           ("so%d" % opar, o_free[opar])],
                                    inc="vec",
                                )
                                pf_free[gpar] = v1
                                yt_free = v1
                                o_free[opar] = em.op(
                                    "scalar",
                                    lambda e, opar=opar, r0=r0, nb=nb, nt=nt: e.dma_start(
                                        out_h[r0:r0 + 128,
                                              nb * 1024 + nt * 512: nb * 1024 + (nt + 1) * 512],
                                        o_st[:, opar]),
                                    waits=[("vec", v1)],
                                    inc="so%d" % opar, amt=16,
                                )
                                git += 1
                em.op("scalar", None,
                      waits=[("so%d" % i, em.cnt["so%d" % i]) for i in range(4)])
            em.flush(nc, sems)

    em.check_deadlock()
    return nc


_CACHE = {}


def kernel(L0_attn_hidden_states, L0_attn_wq, L0_attn_wkv, L0_attn_wo,
           L0_ffn_w12, L0_ffn_b12, L0_ffn_w3, L0_ffn_b3):
    x = np.ascontiguousarray(np.asarray(L0_attn_hidden_states, dtype=np.float32))
    wq = np.asarray(L0_attn_wq, dtype=np.float32)
    wkv = np.asarray(L0_attn_wkv, dtype=np.float32)
    wo = np.asarray(L0_attn_wo, dtype=np.float32)
    w12 = np.asarray(L0_ffn_w12, dtype=np.float32)
    b12 = np.asarray(L0_ffn_b12, dtype=np.float32)
    w3 = np.asarray(L0_ffn_w3, dtype=np.float32)
    b3 = np.asarray(L0_ffn_b3, dtype=np.float32)

    wq_t = np.ascontiguousarray(
        wq.reshape(NKC, 128, H, DQ)[:, :, :, :DN].transpose(1, 2, 0, 3))
    wk_t = np.ascontiguousarray(
        wkv[:, :H * DN].reshape(NKC, 128, H, DN).transpose(1, 2, 0, 3))
    wv_t = np.ascontiguousarray(
        wkv[:, H * DN:H * (DN + DV)].reshape(NKC, 128, H, DV).transpose(1, 2, 0, 3))
    perm = np.concatenate([
        np.concatenate([np.arange(j * 128, (j + 1) * 128),
                        np.arange(4096 + j * 128, 4096 + (j + 1) * 128)])
        for j in range(32)])
    w12_p = w12[perm]
    w12_t = np.ascontiguousarray(
        w12_p.reshape(32, 256, NKC, 128).transpose(3, 0, 2, 1))
    b12_t = np.ascontiguousarray(b12[perm].reshape(64, 128).T)
    w3_t = np.ascontiguousarray(
        w3.reshape(4, 1024, NKC, 128).transpose(3, 0, 2, 1))
    wo_t = np.ascontiguousarray(
        wo.reshape(H, 128, 4, 1024).transpose(1, 2, 0, 3))
    b3_t = np.ascontiguousarray(b3.reshape(1, D))
    p_i = np.arange(128)[:, None]
    f_i = np.arange(512)[None, :]
    masks_t = np.concatenate(
        [(128 * j + p_i <= f_i).astype(ml_dtypes.bfloat16) for j in range(4)], axis=1)
    ident_t = np.eye(128, dtype=ml_dtypes.bfloat16)

    if "nc" not in _CACHE:
        _CACHE["nc"] = build_nc()
    nc = _CACHE["nc"]

    shared = {
        "wq_t": wq_t, "wk_t": wk_t, "wv_t": wv_t, "w12_t": w12_t,
        "b12_t": b12_t, "w3_t": w3_t, "wo_t": wo_t, "b3_t": b3_t,
        "masks_t": masks_t, "ident_t": ident_t,
    }
    in_maps = [dict(shared, x=np.ascontiguousarray(x[c])) for c in range(8)]
    res = run_bass_kernel_spmd(nc, in_maps, core_ids=list(range(8)))
    return np.stack([np.asarray(r["out"]) for r in res.results], axis=0)


# version 18 (final)
# speedup vs baseline: 1.2307x; 1.2307x over previous
import contextlib
import sys
import numpy as np
import ml_dtypes
import concourse.bass as bass
import concourse.mybir as mybir
from concourse.bass_utils import run_bass_kernel_spmd

F32 = mybir.dt.float32
BF16 = mybir.dt.bfloat16
AF = mybir.ActivationFunctionType
ALU = mybir.AluOpType
AX = mybir.AxisListType

S = 2048
D = 4096
H = 32
DN = 128
DV = 128
DQ = 192
S4 = float(DQ) ** -0.25
NTT = S // 512
NKC = D // 128

ENGINES = ("sync", "scalar", "vector", "tensor", "gpsimd")
ALL_SEMS = ("mm", "act", "vec", "ltr0", "ltr1", "lvt0", "lvt1",
            "lid", "lmk", "lb12", "lb3",
            "lx0", "lx1", "lk", "lq", "lv",
            "sa0", "sa1", "lw0", "lw1", "sx0", "sx1", "sx2", "sx3",
            "lw3a", "lw3b", "lw3c", "lw3d", "lx4a", "lx4b", "lxt0", "lxt1",
            "sy0", "sy1", "sy2", "sy3",
            "lwoa", "lwob", "lwoc", "lwod", "laTa", "laTb", "ly0", "ly1",
            "so0", "so1", "so2", "so3")


class Emitter:
    def __init__(self):
        self.cnt = {s: 0 for s in ALL_SEMS}
        self.items = {e: [] for e in ENGINES}
        self.all_items = {e: [] for e in ENGINES}
        self.guaranteed = {e: {} for e in ENGINES}
        self.pending_barrier = None
        self.barrier_done = set()

    def phase_barrier(self):
        self.pending_barrier = dict(self.cnt)
        self.barrier_done = set()

    def op(self, eng, fn, waits=(), inc=None, amt=1):
        waits = list(waits)
        if self.pending_barrier is not None and eng not in self.barrier_done:
            waits += [(s, v) for s, v in self.pending_barrier.items() if v > 0]
            self.barrier_done.add(eng)
            if len(self.barrier_done) == len(ENGINES):
                self.pending_barrier = None
        w = []
        g = self.guaranteed[eng]
        for s, v in waits:
            if v is None or v <= 0:
                continue
            if g.get(s, 0) >= v:
                continue
            g[s] = v
            w.append((s, v))
        if inc is not None:
            self.cnt[inc] += amt
            ret = self.cnt[inc]
        else:
            ret = None
        item = (fn, tuple(w), inc, amt, sys._getframe(1).f_lineno)
        self.items[eng].append(item)
        self.all_items[eng].append(item)
        return ret

    def flush(self, nc, sems):
        if not any(self.items.values()):
            return
        with nc.Block() as block:
            for eng in ENGINES:
                items = self.items[eng]
                if not items:
                    continue

                def body(e, items=items):
                    for fn, waits, inc, amt, _ln in items:
                        for s, v in waits:
                            e.wait_ge(sems[s], v)
                        ins = fn(e) if fn is not None else None
                        if inc is not None:
                            ins.then_inc(sems[inc], amt)

                getattr(block, eng)(body)
        self.items = {e: [] for e in ENGINES}

    def check_deadlock(self):
        cnt = {s: 0 for s in self.cnt}
        heads = {e: 0 for e in ENGINES}
        streams = self.all_items
        while True:
            progress = False
            done = True
            for e in ENGINES:
                st = streams[e]
                while heads[e] < len(st):
                    fn, waits, inc, amt, _ln = st[heads[e]]
                    if all(cnt[s] >= v for s, v in waits):
                        if inc is not None:
                            cnt[inc] += amt
                        heads[e] += 1
                        progress = True
                    else:
                        break
                if heads[e] < len(st):
                    done = False
            if done:
                return
            if not progress:
                info = []
                for e in ENGINES:
                    if heads[e] < len(streams[e]):
                        fn, waits, inc, amt, ln = streams[e][heads[e]]
                        blocked = [(s, v, cnt[s]) for s, v in waits if cnt[s] < v]
                        info.append(f"{e}@{heads[e]}/{len(streams[e])} line{ln} blocked on {blocked}")
                raise RuntimeError("DEADLOCK in emitted program: " + "; ".join(info))


def build_nc():
    nc = bass.Bass("TRN2", target_bir_lowering=False)

    x_h = nc.dram_tensor("x", [S, D], F32, kind="ExternalInput").ap()
    wq_h = nc.dram_tensor("wq_t", [128, H, NKC, 128], F32, kind="ExternalInput").ap()
    wk_h = nc.dram_tensor("wk_t", [128, H, NKC, 128], F32, kind="ExternalInput").ap()
    wv_h = nc.dram_tensor("wv_t", [128, H, NKC, 128], F32, kind="ExternalInput").ap()
    w12_h = nc.dram_tensor("w12_t", [128, 32, NKC, 256], F32, kind="ExternalInput").ap()
    b12_h = nc.dram_tensor("b12_t", [128, 64], F32, kind="ExternalInput").ap()
    w3_h = nc.dram_tensor("w3_t", [128, 4, NKC, 1024], F32, kind="ExternalInput").ap()
    wo_h = nc.dram_tensor("wo_t", [128, 4, H, 1024], F32, kind="ExternalInput").ap()
    b3_h = nc.dram_tensor("b3_t", [1, D], F32, kind="ExternalInput").ap()
    masks_h = nc.dram_tensor("masks_t", [128, 4 * 512], BF16, kind="ExternalInput").ap()
    ident_h = nc.dram_tensor("ident_t", [128, 128], BF16, kind="ExternalInput").ap()
    out_h = nc.dram_tensor("out", [S, D], F32, kind="ExternalOutput").ap()
    attn_scr = nc.dram_tensor("attn_scr", [128, H, S], BF16, kind="Internal").ap()
    x4_scr = nc.dram_tensor("x4_scr", [128, 32, S], BF16, kind="Internal").ap()
    y1_scr = nc.dram_tensor("y1_scr", [S, D], F32, kind="Internal").ap()

    em = Emitter()
    sems = {}
    stack = contextlib.ExitStack()
    for s in ALL_SEMS:
        sems[s] = stack.enter_context(nc.semaphore(s))

    with stack:
        with contextlib.ExitStack() as _pc:
            _e = _pc.enter_context
            ident = _e(nc.sbuf_tensor("ident", [128, 128], BF16))
            ones_col = _e(nc.sbuf_tensor("ones_col", [128, 1], BF16))
            ones_mat = _e(nc.sbuf_tensor("ones_mat", [128, 128], BF16))
            ones_row = _e(nc.sbuf_tensor("ones_row", [1, 128], F32))
            epsb = _e(nc.sbuf_tensor("epsb", [128, 1], F32))
            ld_ident = em.op("sync", lambda e: e.dma_start(ident[:, :], ident_h), inc="lid", amt=16)
            em.op("vector", lambda e: e.memset(ones_col[:, :], 1.0), inc="vec")
            em.op("vector", lambda e: e.memset(ones_mat[:, :], 1.0), inc="vec")
            em.op("vector", lambda e: e.memset(ones_row[:, :], 1.0), inc="vec")
            ms_eps = em.op("vector", lambda e: e.memset(epsb[:, :], 1e-5), inc="vec")

            with nc.sbuf_tensor("xnT", [128, NKC, S], BF16) as xnT:
                with contextlib.ExitStack() as _p0:
                    _e = _p0.enter_context
                    x_in = _e(nc.sbuf_tensor("x_in", [128, 2, D], F32))
                    xc = _e(nc.sbuf_tensor("xc", [128, D], F32))
                    xn_bf = _e(nc.sbuf_tensor("xn_bf", [128, 2, D], BF16))
                    stat = _e(nc.sbuf_tensor("stat", [128, 8], F32))
                    ptrA = _e(nc.psum_tensor("ptrA", [128, 1024], BF16))
                    ptrB = _e(nc.psum_tensor("ptrB", [128, 1024], BF16))

                    xsum = stat[:, 0:1]
                    mneg = stat[:, 1:2]
                    ssq = stat[:, 2:3]
                    sd = stat[:, 3:4]
                    rstd = stat[:, 4:5]
                    xin_free_v = [0, 0]
                    xin_free_a = [0, 0]
                    xnbf_free = [0, 0]
                    ptr_free = {}
                    prev_a3 = 0

                    for i in range(16):
                        par = i % 2
                        ldv = em.op(
                            "sync",
                            lambda e, par=par, i=i: e.dma_start(
                                x_in[:, par], x_h[i * 128:(i + 1) * 128, :]),
                            waits=[("vec", xin_free_v[par]), ("act", xin_free_a[par])],
                            inc="lx%d" % par, amt=16,
                        )
                        v1 = em.op(
                            "vector",
                            lambda e, par=par: e.reduce_sum(xsum, x_in[:, par], axis=AX.X),
                            waits=[("lx%d" % par, ldv), ("act", prev_a3), ("vec", ms_eps)],
                            inc="vec",
                        )
                        a1 = em.op(
                            "scalar",
                            lambda e: e.activation(mneg, xsum, AF.Copy, scale=-1.0 / D),
                            waits=[("vec", v1)],
                            inc="act",
                        )
                        v2 = em.op(
                            "vector",
                            lambda e, par=par: e.tensor_scalar(xc[:, :], x_in[:, par], mneg, None, ALU.add),
                            waits=[("act", a1)],
                            inc="vec",
                        )
                        xin_free_v[par] = v2
                        a2 = em.op(
                            "scalar",
                            lambda e, par=par: e.activation(x_in[:, par], xc[:, :], AF.Square, accum_out=ssq),
                            waits=[("vec", v2)],
                            inc="act",
                        )
                        xin_free_a[par] = a2
                        a3 = em.op(
                            "scalar",
                            lambda e: e.activation(sd, ssq, AF.Sqrt, scale=1.0 / D, bias=epsb[:, :]),
                            waits=[("act", a2)],
                            inc="act",
                        )
                        prev_a3 = a3
                        v3 = em.op(
                            "vector",
                            lambda e: e.reciprocal(rstd, sd),
                            waits=[("act", a3)],
                            inc="vec",
                        )
                        v4 = em.op(
                            "vector",
                            lambda e, par=par: e.tensor_scalar(xn_bf[:, par], xc[:, :], rstd, None, ALU.mult),
                            waits=[("vec", v3), ("act", xnbf_free[par])],
                            inc="vec",
                        )
                        last_t = 0
                        for g in range(NKC // 4):
                            gb = g % 2
                            bank = (ptrA, ptrB)[gb]
                            tg = 0
                            for slot in range(4):
                                kc = g * 4 + slot
                                tg = em.op(
                                    "tensor",
                                    lambda e, bank=bank, slot=slot, par=par, kc=kc: e.transpose(
                                        bank[:, slot * 128:(slot + 1) * 128],
                                        xn_bf[:, par, kc * 128:(kc + 1) * 128],
                                        ident[:, :],
                                    ),
                                    waits=([("vec", v4), ("lid", ld_ident),
                                            ("act", ptr_free.get(gb, 0))] if slot == 0 else ()),
                                    inc="mm" if slot == 3 else None,
                                )
                            ev = em.op(
                                "scalar",
                                lambda e, bank=bank, g=g, i=i: e.copy(
                                    xnT[:, g * 4:(g + 1) * 4, i * 128:(i + 1) * 128],
                                    bank[:, 0:512].rearrange("p (a b) -> p a b", a=4),
                                ),
                                waits=[("mm", tg)],
                                inc="act",
                            )
                            ptr_free[gb] = ev
                            last_t = ev
                        xnbf_free[par] = last_t
                    xnT_done_act = em.cnt["act"]
                em.flush(nc, sems)
                em.phase_barrier()

                with contextlib.ExitStack() as _p1:
                    _e = _p1.enter_context
                    masks = _e(nc.sbuf_tensor("masks", [128, 4 * 512], BF16))
                    ld_masks = em.op("sync", lambda e: e.dma_start(masks[:, :], masks_h),
                                     inc="lmk", amt=16)
                    k_w = _e(nc.sbuf_tensor("k_w", [128, NKC, 128], BF16))
                    q_w = _e(nc.sbuf_tensor("q_w", [128, NKC, 128], BF16))
                    v_w = _e(nc.sbuf_tensor("v_w", [128, NKC, 128], BF16))
                    kT_sb = _e(nc.sbuf_tensor("kT_sb", [128, S], BF16))
                    qT_sb = _e(nc.sbuf_tensor("qT_sb", [128, S], BF16))
                    V_sb = _e(nc.sbuf_tensor("V_sb", [128, S], BF16))
                    sq_sb = _e(nc.sbuf_tensor("sq_sb", [128, S], BF16))
                    tmp4 = _e(nc.sbuf_tensor("tmp4", [128, 2, 512], BF16))
                    expS = _e(nc.sbuf_tensor("expS", [128, 8, 512], BF16))
                    expT = _e(nc.sbuf_tensor("expT", [128, 4, 512], BF16))
                    attnT_st = _e(nc.sbuf_tensor("attnT_st", [128, 2, 512], BF16))
                    recipb = _e(nc.sbuf_tensor("recipb", [128, 512], F32))
                    lnr = _e(nc.sbuf_tensor("lnr", [128, 512], F32))
                    negc = _e(nc.sbuf_tensor("negc", [128, 1], F32))
                    frags = _e(nc.sbuf_tensor("frags", [128, 16], F32))
                    csc = _e(nc.sbuf_tensor("csc", [128, 4], F32))
                    pk0 = _e(nc.psum_tensor("pk0", [128, 512], F32))
                    pk1 = _e(nc.psum_tensor("pk1", [128, 512], F32))
                    ps0 = _e(nc.psum_tensor("ps0", [128, 512], F32))
                    ps1 = _e(nc.psum_tensor("ps1", [128, 512], F32))
                    ppv = _e(nc.psum_tensor("ppv", [128, 512], F32))
                    psb = _e(nc.psum_tensor("psb", [128, 512], F32))
                    ptrV = _e(nc.psum_tensor("ptrV", [128, 1024], BF16))
                    ps3 = _e(nc.psum_tensor("ps3", [128, 512], F32))
                    w_free = {"k": 0, "q": 0, "v": 0}
                    w_ready = {"k": 0, "q": 0, "v": 0}
                    evac_free = {0: 0, 1: 0}
                    vsb_war = [0]
                    vtr_free = {0: 0}
                    tmp4_free = [0, 0]
                    attnT_free = [0, 0]
                    state = {"recipb_war": 0, "psb_free_vec": 0, "psb_free_act": 0}

                    def wload(name, hbm_ap, sb, h):
                        return em.op(
                            "gpsimd",
                            lambda e, hbm_ap=hbm_ap, sb=sb, h=h: e.dma_start(sb[:, :, :], hbm_ap[:, h]),
                            waits=[("mm", w_free[name])],
                            inc="l" + name, amt=16,
                        )

                    w_ready["k"] = wload("k", wk_h, k_w, 0)
                    w_ready["q"] = wload("q", wq_h, q_w, 0)
                    w_ready["v"] = wload("v", wv_h, v_w, 0)

                    def qkv_block(w_sb, rsem, ready, dst, scale, extra_waits,
                                  post_group_hook=None):
                        stop = 0
                        evacs = [0] * NTT
                        pend_tr = [None]

                        def flush_tr():
                            tpar, tt0 = pend_tr[0]
                            pend_tr[0] = None
                            tg = 0
                            for i4 in range(4):
                                tg = em.op(
                                    "tensor",
                                    lambda e, tpar=tpar, i4=i4: e.transpose(
                                        ptrV[:, i4 * 128:(i4 + 1) * 128],
                                        tmp4[:, tpar, i4 * 128:(i4 + 1) * 128],
                                        ident[:, :]),
                                    waits=([("act", evac_free[tt0 % 2]), ("act", vtr_free[0]),
                                            ("mm", vsb_war[0])] if i4 == 0 else ()),
                                    inc="mm" if i4 == 3 else None,
                                )
                            tmp4_free[tpar] = tg
                            vev = em.op(
                                "scalar",
                                lambda e, tt0=tt0: e.copy(
                                    V_sb[:, tt0 * 512:(tt0 + 1) * 512],
                                    ptrV[:, 0:512]),
                                waits=[("mm", tg)],
                                inc="act",
                            )
                            vtr_free[0] = vev

                        for tt in range(NTT):
                            bank = (pk0, pk1)[tt % 2]
                            mv = None
                            for kc in range(NKC):
                                mv = em.op(
                                    "tensor",
                                    lambda e, bank=bank, kc=kc, tt=tt, w_sb=w_sb: e.matmul(
                                        bank[:, :], w_sb[:, kc, :],
                                        xnT[:, kc, tt * 512:(tt + 1) * 512],
                                        start=(kc == 0), stop=(kc == NKC - 1)),
                                    waits=(([(rsem, ready), ("act", xnT_done_act),
                                             ("act", evac_free[tt % 2])]
                                            + (extra_waits if tt == 0 else []))
                                           if kc == 0 else ()),
                                    inc="mm" if kc == NKC - 1 else None,
                                )
                            stop = mv
                            if dst is not None:
                                evacs[tt] = em.op(
                                    "scalar",
                                    lambda e, bank=bank, tt=tt, dst=dst, scale=scale: e.activation(
                                        dst[:, tt * 512:(tt + 1) * 512], bank[:, :],
                                        AF.Copy, scale=scale),
                                    waits=[("mm", mv)],
                                    inc="act",
                                )
                                evac_free[tt % 2] = evacs[tt]
                            else:
                                tpar = tt % 2
                                ev = em.op(
                                    "scalar",
                                    lambda e, bank=bank, tpar=tpar: e.copy(tmp4[:, tpar], bank[:, :]),
                                    waits=[("mm", mv), ("mm", tmp4_free[tpar])],
                                    inc="act",
                                )
                                evac_free[tt % 2] = ev
                                if pend_tr[0] is not None:
                                    flush_tr()
                                pend_tr[0] = (tpar, tt)
                            if post_group_hook is not None:
                                post_group_hook(tt)
                        if pend_tr[0] is not None:
                            flush_tr()
                        return stop, evacs

                    kk_banks = (ps0, ps1, ppv, psb)

                    def norm_bound(src, src_evacs, frag_base, extra_mm_wait, bank_prev_fv,
                                   extra_waits):
                        sqv = [0] * NTT
                        for tt in range(NTT):
                            sqv[tt] = em.op(
                                "vector",
                                lambda e, tt=tt, src=src: e.tensor_mul(
                                    sq_sb[:, tt * 512:(tt + 1) * 512],
                                    src[:, tt * 512:(tt + 1) * 512],
                                    src[:, tt * 512:(tt + 1) * 512]),
                                waits=[("act", src_evacs[tt]), ("mm", extra_mm_wait)],
                                inc="vec",
                            )
                        mms = [0] * NTT
                        for tt in range(NTT):
                            mms[tt] = em.op(
                                "tensor",
                                lambda e, tt=tt: e.matmul(
                                    kk_banks[tt][:, :], ones_mat[:, :],
                                    sq_sb[:, tt * 512:(tt + 1) * 512], start=True, stop=True),
                                waits=[("vec", sqv[tt]), ("vec", bank_prev_fv[tt])] + extra_waits,
                                inc="mm",
                            )
                        fv = [0] * NTT
                        for tt in range(NTT):
                            fv[tt] = em.op(
                                "vector",
                                lambda e, tt=tt, frag_base=frag_base: e.reduce_max(
                                    frags[:, frag_base + tt:frag_base + tt + 1],
                                    kk_banks[tt][:, :], axis=AX.X),
                                waits=[("mm", mms[tt])],
                                inc="vec",
                            )
                        return mms, fv

                    sbank_free_vec = [0, 0]
                    expT_free_v = 0
                    prev_fv = [0] * NTT
                    prev_qqm = 0
                    sb_rot = (ps0, ps1, pk0, ps3)
                    NSB = 4
                    pv_rot = (ppv, pk1)
                    sb_free_act = {0: 0, 1: 0, 2: 0, 3: 0}
                    sb_free_vec = {0: 0, 1: 0, 2: 0, 3: 0}
                    pv_free = {0: 0, 1: 0}
                    pv_free_act = {0: 0, 1: 0}
                    expT_free_v = [0, 0, 0, 0]
                    l1_free = {0: 0}
                    prev_fv = [0] * NTT
                    prev_qqm = 0
                    qtg = 0

                    for h in range(H):
                        k_extra = [("act", sb_free_act[2]), ("vec", pv_free[0]),
                                   ("vec", pv_free[1])]
                        k_stop, k_evac = qkv_block(k_w, "lk", w_ready["k"], kT_sb, S4, k_extra)
                        w_free["k"] = k_stop
                        if h + 1 < H:
                            k_next = wload("k", wk_h, k_w, h + 1)
                        q_stop, q_evac = qkv_block(q_w, "lq", w_ready["q"], qT_sb, S4, [])
                        w_free["q"] = q_stop
                        if h + 1 < H:
                            q_next = wload("q", wq_h, q_w, h + 1)
                        bank_extra = [("act", sb_free_act[0]), ("act", sb_free_act[1]),
                                      ("vec", pv_free[0]), ("act", l1_free[0])]
                        kkm, kk_fv = norm_bound(kT_sb, k_evac, 0, prev_qqm, prev_fv, bank_extra)
                        mkk_v = em.op(
                            "vector",
                            lambda e: e.reduce_max(csc[:, 0:1], frags[:, 0:4], axis=AX.X),
                            waits=[("vec", kk_fv[-1])],
                            inc="vec",
                        )
                        qq_sqv = [0] * NTT
                        for tt in range(NTT):
                            qq_sqv[tt] = em.op(
                                "vector",
                                lambda e, tt=tt: e.tensor_mul(
                                    sq_sb[:, tt * 512:(tt + 1) * 512],
                                    qT_sb[:, tt * 512:(tt + 1) * 512],
                                    qT_sb[:, tt * 512:(tt + 1) * 512]),
                                waits=[("act", q_evac[tt]), ("mm", kkm[-1])],
                                inc="vec",
                            )
                        qqm = [0] * NTT
                        qq_fv = [0] * NTT

                        def qq_hook(tt):
                            qqm[tt] = em.op(
                                "tensor",
                                lambda e, tt=tt: e.matmul(
                                    kk_banks[tt][:, :], ones_mat[:, :],
                                    sq_sb[:, tt * 512:(tt + 1) * 512], start=True, stop=True),
                                waits=[("vec", qq_sqv[tt]), ("vec", kk_fv[tt])] + bank_extra,
                                inc="mm",
                            )
                            qq_fv[tt] = em.op(
                                "vector",
                                lambda e, tt=tt: e.reduce_max(
                                    frags[:, 4 + tt:5 + tt], kk_banks[tt][:, :], axis=AX.X),
                                waits=[("mm", qqm[tt])],
                                inc="vec",
                            )

                        v_stop, _ = qkv_block(v_w, "lv", w_ready["v"], None, 1.0, [],
                                              post_group_hook=qq_hook)
                        w_free["v"] = v_stop
                        if h + 1 < H:
                            v_next = wload("v", wv_h, v_w, h + 1)
                        v_rdy_act = em.cnt["act"]
                        prev_fv = qq_fv
                        prev_qqm = qqm[-1]
                        sb_free_vec = {0: qq_fv[0], 1: qq_fv[1], 2: 0, 3: 0}
                        sb_free_act[2] = max(sb_free_act[2], evac_free[0])
                        pv_free[0] = max(pv_free[0], qq_fv[2])
                        pv_free_act = {0: 0, 1: evac_free[1]}
                        state["psb_free_vec"] = max(state["psb_free_vec"], qq_fv[3])
                        mqq_v = em.op(
                            "vector",
                            lambda e: e.reduce_max(csc[:, 1:2], frags[:, 4:8], axis=AX.X),
                            waits=[("vec", qq_fv[-1])],
                            inc="vec",
                        )
                        csum_v = em.op(
                            "vector",
                            lambda e: e.tensor_add(csc[:, 2:3], csc[:, 0:1], csc[:, 1:2]),
                            waits=[("vec", mqq_v), ("vec", mkk_v)],
                            inc="vec",
                        )
                        negc_a = em.op(
                            "scalar",
                            lambda e: e.activation(negc[:, :], csc[:, 2:3], AF.Copy, scale=-0.5),
                            waits=[("vec", csum_v)],
                            inc="act",
                        )

                        LOOK = 6
                        for qt in range(NTT):
                            nchunks = (qt + 1) * 4
                            order = list(range(nchunks - 1, -1, -1))
                            pvp = (qt + 1) % 2
                            pvb = pv_rot[pvp]
                            smb = psb
                            rdy_list = [None] * nchunks
                            sums_last = 0
                            pv_last = 0
                            for step in range(nchunks + LOOK):
                                if step < nchunks:
                                    ii = step
                                    kc = order[ii]
                                    b = ii % NSB
                                    sbank = sb_rot[b]
                                    epar = ii % 8
                                    smm = em.op(
                                        "tensor",
                                        lambda e, sbank=sbank, kc=kc, qt=qt: e.matmul(
                                            sbank[:, :], kT_sb[:, kc * 128:(kc + 1) * 128],
                                            qT_sb[:, qt * 512:(qt + 1) * 512],
                                            start=True, stop=True),
                                        waits=[("act", sb_free_act[b]), ("vec", sb_free_vec[b]),
                                               ("act", k_evac[kc // 4]), ("act", q_evac[qt])],
                                        inc="mm",
                                    )
                                    if kc >= qt * 4:
                                        j = kc - qt * 4
                                        etp = ii % 4
                                        ex = em.op(
                                            "scalar",
                                            lambda e, sbank=sbank, etp=etp: e.activation(
                                                expT[:, etp], sbank[:, :], AF.Exp,
                                                bias=negc[:, :]),
                                            waits=[("mm", smm), ("act", negc_a),
                                                   ("vec", expT_free_v[etp])],
                                            inc="act",
                                        )
                                        sb_free_act[b] = ex
                                        rdy = em.op(
                                            "vector",
                                            lambda e, epar=epar, j=j, etp=etp: e.tensor_mul(
                                                expS[:, epar], expT[:, etp],
                                                masks[:, j * 512:(j + 1) * 512]),
                                            waits=[("act", ex), ("lmk", ld_masks)],
                                            inc="vec",
                                        )
                                        expT_free_v[etp] = rdy
                                        rdy_list[ii] = ("vec", rdy)
                                    else:
                                        rdy = em.op(
                                            "scalar",
                                            lambda e, sbank=sbank, epar=epar: e.activation(
                                                expS[:, epar], sbank[:, :], AF.Exp,
                                                bias=negc[:, :]),
                                            waits=[("mm", smm), ("act", negc_a)],
                                            inc="act",
                                        )
                                        sb_free_act[b] = rdy
                                        rdy_list[ii] = ("act", rdy)
                                jj = step - LOOK
                                if jj >= 0:
                                    kcj = order[jj]
                                    epj = jj % 8
                                    first = jj == 0
                                    last = jj == nchunks - 1
                                    sums_last = em.op(
                                        "tensor",
                                        lambda e, epj=epj, first=first, last=last, smb=smb: e.matmul(
                                            smb[:, :], ones_mat[:, :], expS[:, epj],
                                            start=first, stop=last),
                                        waits=([rdy_list[jj], ("act", l1_free[0])] if first
                                               else [rdy_list[jj]]),
                                        inc="mm" if last else None,
                                    )
                                    pv_last = em.op(
                                        "tensor",
                                        lambda e, pvb=pvb, epj=epj, kcj=kcj, first=first,
                                               last=last: e.matmul(
                                            pvb[:, :], V_sb[:, kcj * 128:(kcj + 1) * 128],
                                            expS[:, epj], start=first, stop=last),
                                        waits=([("vec", pv_free[pvp]),
                                                ("act", pv_free_act[pvp]),
                                                ("act", v_rdy_act)]
                                               if first else ()),
                                        inc="mm" if last else None,
                                    )
                            l1 = em.op(
                                "scalar",
                                lambda e, smb=smb: e.activation(lnr[:, :], smb[:, :], AF.Ln),
                                waits=[("mm", sums_last)],
                                inc="act",
                            )
                            l1_free[0] = l1
                            r2 = em.op(
                                "scalar",
                                lambda e: e.activation(recipb[:, :], lnr[:, :], AF.Exp, scale=-1.0),
                                waits=[("act", l1), ("vec", state["recipb_war"])],
                                inc="act",
                            )
                            apar = qt % 2
                            nrm = em.op(
                                "vector",
                                lambda e, apar=apar, pvb=pvb: e.tensor_mul(
                                    attnT_st[:, apar], pvb[:, :], recipb[:, :]),
                                waits=[("act", r2), ("mm", pv_last),
                                       ("sa%d" % apar, attnT_free[apar])],
                                inc="vec",
                            )
                            state["recipb_war"] = nrm
                            pv_free[pvp] = nrm
                            attnT_free[apar] = em.op(
                                "scalar",
                                lambda e, apar=apar, h=h, qt=qt: e.dma_start(
                                    attn_scr[:, h, qt * 512:(qt + 1) * 512], attnT_st[:, apar]),
                                waits=[("vec", nrm)],
                                inc="sa%d" % apar, amt=16,
                            )
                            qtg += 1
                        vsb_war[0] = em.cnt["mm"]
                        if h + 1 < H:
                            w_ready["k"] = k_next
                            w_ready["q"] = q_next
                            w_ready["v"] = v_next
                    p1_sa = (em.cnt["sa0"], em.cnt["sa1"])
                em.flush(nc, sems)
                em.phase_barrier()

                with contextlib.ExitStack() as _p2a:
                    _e = _p2a.enter_context
                    w12_bf = _e(nc.sbuf_tensor("w12_bf", [128, 2, NKC, 256], BF16))
                    b12_sb = _e(nc.sbuf_tensor("b12_sb", [128, 64], F32))
                    x4a = _e(nc.sbuf_tensor("x4a", [128, 4, 512], F32))
                    x4_st = _e(nc.sbuf_tensor("x4_st", [128, 4, 512], BF16))
                    pa0 = _e(nc.psum_tensor("pa0", [128, 512], F32))
                    pa1 = _e(nc.psum_tensor("pa1", [128, 512], F32))
                    pa2 = _e(nc.psum_tensor("pa2", [128, 512], F32))
                    pa3 = _e(nc.psum_tensor("pa3", [128, 512], F32))
                    pb0 = _e(nc.psum_tensor("pb0", [128, 512], F32))
                    pb1 = _e(nc.psum_tensor("pb1", [128, 512], F32))
                    pb2 = _e(nc.psum_tensor("pb2", [128, 512], F32))
                    pb3 = _e(nc.psum_tensor("pb3", [128, 512], F32))
                    ld_b12 = em.op("sync", lambda e: e.dma_start(b12_sb[:, :], b12_h),
                                   inc="lb12", amt=16)
                    w12_free = [0, 0]
                    w12_ready = [0, 0]
                    silu_free = [0, 0, 0, 0]
                    spill_free = [0, 0, 0, 0]
                    pA_free = [0, 0, 0, 0]
                    pB_free = [0, 0, 0, 0]
                    git = 0
                    for j in range(32):
                        par = j % 2
                        w12_ready[par] = em.op(
                            "gpsimd",
                            lambda e, par=par, j=j: e.dma_start(w12_bf[:, par], w12_h[:, j]),
                            waits=[("mm", w12_free[par])],
                            inc="lw%d" % par, amt=16,
                        )
                        w12_last = 0
                        for tt in range(NTT):
                            gpar = git % 4
                            pA = (pa0, pa1, pa2, pa3)[gpar]
                            pB = (pb0, pb1, pb2, pb3)[gpar]
                            m1 = None
                            for kc in range(NKC):
                                m1 = em.op(
                                    "tensor",
                                    lambda e, pA=pA, par=par, kc=kc, tt=tt: e.matmul(
                                        pA[:, :], w12_bf[:, par, kc, 0:128],
                                        xnT[:, kc, tt * 512:(tt + 1) * 512],
                                        start=(kc == 0), stop=(kc == NKC - 1)),
                                    waits=([("lw%d" % par, w12_ready[par]), ("act", pA_free[gpar])]
                                           if kc == 0 else ()),
                                    inc="mm" if kc == NKC - 1 else None,
                                )
                            m2 = None
                            for kc in range(NKC):
                                m2 = em.op(
                                    "tensor",
                                    lambda e, pB=pB, par=par, kc=kc, tt=tt: e.matmul(
                                        pB[:, :], w12_bf[:, par, kc, 128:256],
                                        xnT[:, kc, tt * 512:(tt + 1) * 512],
                                        start=(kc == 0), stop=(kc == NKC - 1)),
                                    waits=([("vec", pB_free[gpar])] if kc == 0 else ()),
                                    inc="mm" if kc == NKC - 1 else None,
                                )
                            w12_last = m2
                            spar = git % 4
                            sa = em.op(
                                "scalar",
                                lambda e, pA=pA, spar=spar, j=j: e.activation(
                                    x4a[:, spar], pA[:, :], AF.Silu,
                                    bias=b12_sb[:, 2 * j:2 * j + 1]),
                                waits=[("mm", m1), ("vec", silu_free[spar]), ("lb12", ld_b12)],
                                inc="act",
                            )
                            pA_free[gpar] = sa
                            sv = em.op(
                                "vector",
                                lambda e, pB=pB, spar=spar, j=j: e.scalar_tensor_tensor(
                                    x4_st[:, spar], pB[:, :], b12_sb[:, 2 * j + 1:2 * j + 2],
                                    x4a[:, spar], ALU.add, ALU.mult),
                                waits=[("mm", m2), ("act", sa), ("sx%d" % spar, spill_free[spar])],
                                inc="vec",
                            )
                            silu_free[spar] = sv
                            pB_free[gpar] = sv
                            spill_free[spar] = em.op(
                                "scalar",
                                lambda e, spar=spar, j=j, tt=tt: e.dma_start(
                                    x4_scr[:, j, tt * 512:(tt + 1) * 512], x4_st[:, spar]),
                                waits=[("vec", sv)],
                                inc="sx%d" % spar, amt=16,
                            )
                            git += 1
                        w12_free[par] = w12_last
                    p2a_sx = tuple(em.cnt["sx%d" % i] for i in range(4))
                em.flush(nc, sems)
                em.phase_barrier()

            with contextlib.ExitStack() as _p2b:
                _e = _p2b.enter_context
                w3_bf = _e(nc.sbuf_tensor("w3_bf", [128, NKC, 1024], BF16))
                x4T_tt = _e(nc.sbuf_tensor("x4T_tt", [128, 2, 32, 512], BF16))
                b3b = _e(nc.sbuf_tensor("b3b", [128, 1024], F32))
                b3_row = _e(nc.sbuf_tensor("b3_row", [1, 512], F32))
                x_t = _e(nc.sbuf_tensor("x_t", [128, 1024], F32))
                y_st = _e(nc.sbuf_tensor("y_st", [128, 2, 512], F32))
                pe0 = _e(nc.psum_tensor("pe0", [128, 512], F32))
                pe1 = _e(nc.psum_tensor("pe1", [128, 512], F32))
                pe2 = _e(nc.psum_tensor("pe2", [128, 512], F32))
                pe3 = _e(nc.psum_tensor("pe3", [128, 512], F32))
                pbc = _e(nc.psum_tensor("pbc", [128, 512], F32))
                NB = 4
                b3_state = {"a": 0, "m": 0, "war": 0}

                def build_b3b(nb):
                    for k4 in range(2):
                        k8 = nb * 2 + k4
                        ld_b3 = em.op(
                            "sync",
                            lambda e, k8=k8: e.dma_start(
                                b3_row[:, :], b3_h[0:1, k8 * 512:(k8 + 1) * 512]),
                            waits=[("mm", b3_state["m"])],
                            inc="lb3", amt=16)
                        bm = em.op(
                            "tensor",
                            lambda e: e.matmul(
                                pbc[:, :], ones_row[:, :], b3_row[:, :],
                                start=True, stop=True),
                            waits=[("lb3", ld_b3), ("act", b3_state["a"]),
                                   ("vec", b3_state["war"])],
                            inc="mm",
                        )
                        b3_state["m"] = bm
                        b3_state["a"] = em.op(
                            "scalar",
                            lambda e, k4=k4: e.copy(b3b[:, k4 * 512:(k4 + 1) * 512], pbc[:, :]),
                            waits=[("mm", bm)],
                            inc="act",
                        )
                    return b3_state["a"]

                b3b_done = build_b3b(0)
                w3sem = ("lw3a", "lw3b")
                w3_ready = {}
                w3_war = {}
                x4_war = [0, 0]
                x4_ready = [0, 0]
                xt_ready = 0
                xt_free = 0
                y_free = [0, 0]
                pe_free = [0, 0, 0, 0]
                git = 0

                def load_w3_cols(nb):
                    for nt in range(2):
                        last = 0
                        for j in range(NKC):
                            last = em.op(
                                "gpsimd",
                                lambda e, nb=nb, nt=nt, j=j: e.dma_start(
                                    w3_bf[:, j, nt * 512:(nt + 1) * 512],
                                    w3_h[:, nb, j, nt * 512:(nt + 1) * 512]),
                                waits=[("mm", w3_war.get(nt, 0))],
                                inc=w3sem[nt], amt=16,
                            )
                        w3_ready[(nb, nt)] = last

                load_w3_cols(0)
                last_v1 = 0
                for nb in range(NB):
                    if nb > 0:
                        load_w3_cols(nb)
                        b3_state["war"] = last_v1
                        b3b_done = build_b3b(nb)
                    for tt in range(NTT):
                        xp = (nb * NTT + tt) % 2
                        last = 0
                        for j in range(32):
                            last = em.op(
                                "sync",
                                lambda e, j=j, tt=tt, xp=xp: e.dma_start(
                                    x4T_tt[:, xp, j, :], x4_scr[:, j, tt * 512:(tt + 1) * 512]),
                                waits=[("sx0", p2a_sx[0]), ("sx1", p2a_sx[1]),
                                       ("sx2", p2a_sx[2]), ("sx3", p2a_sx[3]),
                                       ("mm", x4_war[xp])],
                                inc=("lx4a", "lx4b")[xp], amt=16,
                            )
                        x4_ready[xp] = last
                        for ts in range(4):
                            r0 = tt * 512 + ts * 128
                            xt_ready = em.op(
                                "sync",
                                lambda e, r0=r0, nb=nb: e.dma_start(
                                    x_t[:, :], x_h[r0:r0 + 128, nb * 1024:(nb + 1) * 1024]),
                                waits=[("vec", xt_free)],
                                inc="lxt0", amt=16,
                            )
                            for nt in range(2):
                                gpar = git % 4
                                pe = (pe0, pe1, pe2, pe3)[gpar]
                                mv = None
                                for j in range(NKC):
                                    mv = em.op(
                                        "tensor",
                                        lambda e, pe=pe, j=j, ts=ts, nt=nt, xp=xp: e.matmul(
                                            pe[:, :], x4T_tt[:, xp, j, ts * 128:(ts + 1) * 128],
                                            w3_bf[:, j, nt * 512:(nt + 1) * 512],
                                            start=(j == 0), stop=(j == NKC - 1)),
                                        waits=([(("lx4a", "lx4b")[xp], x4_ready[xp]),
                                                (w3sem[nt], w3_ready[(nb, nt)]),
                                                ("vec", pe_free[gpar])] if j == 0 else ()),
                                        inc="mm" if j == NKC - 1 else None,
                                    )
                                if nb < NB - 1:
                                    w3_war[nt] = mv
                                x4_war[xp] = mv
                                ypar = git % 2
                                v1 = em.op(
                                    "vector",
                                    lambda e, pe=pe, ypar=ypar, nt=nt: e.tensor_add(
                                        y_st[:, ypar], pe[:, :],
                                        b3b[:, nt * 512:(nt + 1) * 512]),
                                    waits=[("mm", mv), ("act", b3b_done),
                                           ("sy%d" % ypar, y_free[ypar])],
                                    inc="vec",
                                )
                                last_v1 = v1
                                v2 = em.op(
                                    "vector",
                                    lambda e, ypar=ypar, nt=nt: e.tensor_add(
                                        y_st[:, ypar], y_st[:, ypar],
                                        x_t[:, nt * 512:(nt + 1) * 512]),
                                    waits=[("vec", v1), ("lxt0", xt_ready)],
                                    inc="vec",
                                )
                                pe_free[gpar] = v1
                                xt_free = v2
                                y_free[ypar] = em.op(
                                    "scalar",
                                    lambda e, ypar=ypar, r0=r0, nb=nb, nt=nt: e.dma_start(
                                        y1_scr[r0:r0 + 128,
                                               nb * 1024 + nt * 512: nb * 1024 + (nt + 1) * 512],
                                        y_st[:, ypar]),
                                    waits=[("vec", v2)],
                                    inc="sy%d" % ypar, amt=16,
                                )
                                git += 1
                p2b_sy = tuple(em.cnt["sy%d" % i] for i in range(4))
            em.flush(nc, sems)
            em.phase_barrier()

            with contextlib.ExitStack() as _p3:
                _e = _p3.enter_context
                wo_bf = _e(nc.sbuf_tensor("wo_bf", [128, H, 1024], BF16))
                aT_tt = _e(nc.sbuf_tensor("aT_tt", [128, 2, H, 512], BF16))
                y1_t = _e(nc.sbuf_tensor("y1_t", [128, 1024], F32))
                o_st = _e(nc.sbuf_tensor("o_st", [128, 2, 512], F32))
                pf0 = _e(nc.psum_tensor("pf0", [128, 512], F32))
                pf1 = _e(nc.psum_tensor("pf1", [128, 512], F32))
                pf2 = _e(nc.psum_tensor("pf2", [128, 512], F32))
                pf3 = _e(nc.psum_tensor("pf3", [128, 512], F32))
                NB = 4
                wosem = ("lwoa", "lwob")
                wo_ready = {}
                wo_war = {}
                a_war = [0, 0]
                a_ready = [0, 0]
                yt_ready = 0
                yt_free = 0
                o_free = [0, 0]
                pf_free = [0, 0, 0, 0]
                git = 0

                def load_wo_cols(nb):
                    for nt in range(2):
                        last = 0
                        for hh in range(H):
                            last = em.op(
                                "gpsimd",
                                lambda e, nb=nb, nt=nt, hh=hh: e.dma_start(
                                    wo_bf[:, hh, nt * 512:(nt + 1) * 512],
                                    wo_h[:, nb, hh, nt * 512:(nt + 1) * 512]),
                                waits=[("mm", wo_war.get(nt, 0))],
                                inc=wosem[nt], amt=16,
                            )
                        wo_ready[(nb, nt)] = last

                load_wo_cols(0)
                for nb in range(NB):
                    if nb > 0:
                        load_wo_cols(nb)
                    for tt in range(NTT):
                        xp = (nb * NTT + tt) % 2
                        last = 0
                        for hh in range(H):
                            last = em.op(
                                "sync",
                                lambda e, hh=hh, tt=tt, xp=xp: e.dma_start(
                                    aT_tt[:, xp, hh, :], attn_scr[:, hh, tt * 512:(tt + 1) * 512]),
                                waits=[("sa0", p1_sa[0]), ("sa1", p1_sa[1]), ("mm", a_war[xp])],
                                inc=("laTa", "laTb")[xp], amt=16,
                            )
                        a_ready[xp] = last
                        for ts in range(4):
                            r0 = tt * 512 + ts * 128
                            yt_ready = em.op(
                                "sync",
                                lambda e, r0=r0, nb=nb: e.dma_start(
                                    y1_t[:, :], y1_scr[r0:r0 + 128, nb * 1024:(nb + 1) * 1024]),
                                waits=[("vec", yt_free), ("sy0", p2b_sy[0]),
                                       ("sy1", p2b_sy[1]), ("sy2", p2b_sy[2]),
                                       ("sy3", p2b_sy[3])],
                                inc="ly0", amt=16,
                            )
                            for nt in range(2):
                                gpar = git % 4
                                pf = (pf0, pf1, pf2, pf3)[gpar]
                                mv = None
                                for hh in range(H):
                                    mv = em.op(
                                        "tensor",
                                        lambda e, pf=pf, hh=hh, ts=ts, nt=nt, xp=xp: e.matmul(
                                            pf[:, :], aT_tt[:, xp, hh, ts * 128:(ts + 1) * 128],
                                            wo_bf[:, hh, nt * 512:(nt + 1) * 512],
                                            start=(hh == 0), stop=(hh == H - 1)),
                                        waits=([(("laTa", "laTb")[xp], a_ready[xp]),
                                                (wosem[nt], wo_ready[(nb, nt)]),
                                                ("vec", pf_free[gpar])] if hh == 0 else ()),
                                        inc="mm" if hh == H - 1 else None,
                                    )
                                if nb < NB - 1:
                                    wo_war[nt] = mv
                                a_war[xp] = mv
                                opar = git % 2
                                v1 = em.op(
                                    "vector",
                                    lambda e, pf=pf, opar=opar, nt=nt: e.tensor_add(
                                        o_st[:, opar], pf[:, :],
                                        y1_t[:, nt * 512:(nt + 1) * 512]),
                                    waits=[("mm", mv), ("ly0", yt_ready),
                                           ("so%d" % opar, o_free[opar])],
                                    inc="vec",
                                )
                                pf_free[gpar] = v1
                                yt_free = v1
                                o_free[opar] = em.op(
                                    "scalar",
                                    lambda e, opar=opar, r0=r0, nb=nb, nt=nt: e.dma_start(
                                        out_h[r0:r0 + 128,
                                              nb * 1024 + nt * 512: nb * 1024 + (nt + 1) * 512],
                                        o_st[:, opar]),
                                    waits=[("vec", v1)],
                                    inc="so%d" % opar, amt=16,
                                )
                                git += 1
                em.op("scalar", None,
                      waits=[("so%d" % i, em.cnt["so%d" % i]) for i in range(4)])
            em.flush(nc, sems)

    em.check_deadlock()
    return nc


_CACHE = {}


def kernel(L0_attn_hidden_states, L0_attn_wq, L0_attn_wkv, L0_attn_wo,
           L0_ffn_w12, L0_ffn_b12, L0_ffn_w3, L0_ffn_b3):
    x = np.ascontiguousarray(np.asarray(L0_attn_hidden_states, dtype=np.float32))
    wq = np.asarray(L0_attn_wq, dtype=np.float32)
    wkv = np.asarray(L0_attn_wkv, dtype=np.float32)
    wo = np.asarray(L0_attn_wo, dtype=np.float32)
    w12 = np.asarray(L0_ffn_w12, dtype=np.float32)
    b12 = np.asarray(L0_ffn_b12, dtype=np.float32)
    w3 = np.asarray(L0_ffn_w3, dtype=np.float32)
    b3 = np.asarray(L0_ffn_b3, dtype=np.float32)

    wq_t = np.ascontiguousarray(
        wq.reshape(NKC, 128, H, DQ)[:, :, :, :DN].transpose(1, 2, 0, 3))
    wk_t = np.ascontiguousarray(
        wkv[:, :H * DN].reshape(NKC, 128, H, DN).transpose(1, 2, 0, 3))
    wv_t = np.ascontiguousarray(
        wkv[:, H * DN:H * (DN + DV)].reshape(NKC, 128, H, DV).transpose(1, 2, 0, 3))
    perm = np.concatenate([
        np.concatenate([np.arange(j * 128, (j + 1) * 128),
                        np.arange(4096 + j * 128, 4096 + (j + 1) * 128)])
        for j in range(32)])
    w12_p = w12[perm]
    w12_t = np.ascontiguousarray(
        w12_p.reshape(32, 256, NKC, 128).transpose(3, 0, 2, 1))
    b12_t = np.ascontiguousarray(b12[perm].reshape(64, 128).T)
    w3_t = np.ascontiguousarray(
        w3.reshape(4, 1024, NKC, 128).transpose(3, 0, 2, 1))
    wo_t = np.ascontiguousarray(
        wo.reshape(H, 128, 4, 1024).transpose(1, 2, 0, 3))
    b3_t = np.ascontiguousarray(b3.reshape(1, D))
    p_i = np.arange(128)[:, None]
    f_i = np.arange(512)[None, :]
    masks_t = np.concatenate(
        [(128 * j + p_i <= f_i).astype(ml_dtypes.bfloat16) for j in range(4)], axis=1)
    ident_t = np.eye(128, dtype=ml_dtypes.bfloat16)

    if "nc" not in _CACHE:
        _CACHE["nc"] = build_nc()
    nc = _CACHE["nc"]

    shared = {
        "wq_t": wq_t, "wk_t": wk_t, "wv_t": wv_t, "w12_t": w12_t,
        "b12_t": b12_t, "w3_t": w3_t, "wo_t": wo_t, "b3_t": b3_t,
        "masks_t": masks_t, "ident_t": ident_t,
    }
    in_maps = [dict(shared, x=np.ascontiguousarray(x[c])) for c in range(8)]
    res = run_bass_kernel_spmd(nc, in_maps, core_ids=list(range(8)))
    return np.stack([np.asarray(r["out"]) for r in res.results], axis=0)
